# version 08 (final)
# baseline (speedup 1.0000x reference)
import numpy as np
import ml_dtypes
import concourse.bass as bass
import concourse.mybir as mybir
from concourse.bass_utils import run_bass_kernel_spmd
from contextlib import ExitStack

F32 = mybir.dt.float32
BF16 = mybir.dt.bfloat16
AF = mybir.ActivationFunctionType
ALU = mybir.AluOpType
AX = mybir.AxisListType

B, S, DM = 4, 4096, 2048
INTER = 5632
NIT = INTER // 128
PAIRS = 16
TOK_OWN = 2048
NEG = -1.0e9
EPS = 1e-5
B2 = [0.0625, 0.5625, 1.5625, 3.0625, 6.25, 12.25, 25.0]
QW = [0.5, 0.5, 0.5, 0.5, 1.0, 1.0, 2.0]

_CACHE = {}


class Ctr:
    def __init__(self, handle):
        self.h = handle
        self.n = 0

    def i(self, inst):
        self.n += 1
        inst.then_inc(self.h, 1)
        return self.n

    def d(self, inst):
        self.n += 16
        inst.then_inc(self.h, 16)
        return self.n

    def w(self, eng):
        eng.wait_ge(self.h, self.n)


def build_program():
    nc = bass.Bass("TRN2", target_bir_lowering=False, debug=False)
    es = ExitStack()

    def din(name, shape, dt=F32):
        return nc.dram_tensor(name, shape, dt, kind="ExternalInput").ap()

    x_perm = din("x_perm", [S, DM])
    w_in = {"q": din("wq_in", [DM, DM]), "k": din("wk_in", [DM, DM]),
            "v": din("wv_in", [DM, DM]), "o": din("wo_in", [DM, DM])}
    gwT_in = din("gwT_in", [DM, INTER])
    uwT_in = din("uwT_in", [DM, INTER])
    dwT_in = din("dwT_in", [INTER, DM])
    gs_cols = din("gs_cols", [128, NIT])
    us_cols = din("us_cols", [128, NIT])
    ds_bcast = din("ds_bcast", [128, DM])
    cosq_in = din("cosq_in", [128, TOK_OWN])
    sinq_in = din("sinq_in", [128, TOK_OWN])
    cosk_in = din("cosk_in", [128, S])
    sink_in = din("sink_in", [128, S])
    mask_in = din("mask_in", [128, 4 * 512])
    perm2_in = din("perm2_in", [128, 128], BF16)
    idb_in = din("idb_in", [128, 128], BF16)
    idf_in = din("idf_in", [128, 128], F32)
    ones_in = din("ones_in", [1, S], BF16)

    y_out = nc.dram_tensor("y", [TOK_OWN, DM], F32, kind="ExternalOutput").ap()

    def scr(name, shape, dt):
        return nc.dram_tensor(name, shape, dt, kind="Internal").ap()

    xq_s = scr("xq_s", [S, DM], BF16)
    w_s = {w: scr(f"w{w}_s", [DM, DM], BF16) for w in "qkvo"}
    q_s = scr("q_s", [PAIRS, 128, TOK_OWN], BF16)
    k_s = scr("k_s", [PAIRS, 128, S], BF16)
    v_s = scr("v_s", [PAIRS, 160, S], BF16)
    h1n_s = scr("h1n_s", [TOK_OWN, DM], F32)
    h1nb_s = scr("h1nb_s", [TOK_OWN, DM], BF16)
    gwb_s = scr("gwb_s", [DM, INTER], BF16)
    uwb_s = scr("uwb_s", [DM, INTER], BF16)
    dwb_s = scr("dwb_s", [INTER, DM], BF16)

    _QNAMES = ["x", "w", "wt", "cast", "p0", "p1", "xl0", "xl1", "ffw0",
               "ffw1", "dw0", "dw1", "acc", "y0", "y1",
               "xq", "xqT", "wst", "vqk", "woT", "h1", "sgq", "sdc"]
    Q = {n: Ctr(es.enter_context(nc.semaphore("q_" + n))) for n in _QNAMES}
    SGQ = Q["sgq"]
    SDc = Q["sdc"]
    SDp = [Q["p0"], Q["p1"]]
    SDacc = Q["acc"]
    V = Ctr(es.enter_context(nc.semaphore("V")))
    T = Ctr(es.enter_context(nc.semaphore("T")))
    SC = Ctr(es.enter_context(nc.semaphore("SC")))
    G = Ctr(es.enter_context(nc.semaphore("G")))

    def sbuf(stack, name, shape, dt):
        return stack.enter_context(nc.sbuf_tensor(name, shape, dt))

    def psum(stack, name, shape, dt=F32):
        return stack.enter_context(nc.psum_tensor(name, shape, dt))

    perm2 = sbuf(es, "perm2", [128, 128], BF16)
    idb = sbuf(es, "idb", [128, 128], BF16)
    idf = sbuf(es, "idf", [128, 128], F32)
    gsc = sbuf(es, "gsc", [128, NIT], F32)
    usc = sbuf(es, "usc", [128, NIT], F32)
    dsb = sbuf(es, "dsb", [128, DM], F32)
    epsc = sbuf(es, "epsc", [128, 1], F32)
    masks = sbuf(es, "masks", [128, 4 * 512], F32)

    blk = es.enter_context(nc.Block())

    def ph_eps(vector):
        V.i(vector.memset(epsc[:], EPS))
    blk.vector(ph_eps)

    def q_amax(eng, ctr, src_f32, tmps, F, off):
        nb = F // 16
        amax, s6 = tmps["amax"], tmps["s6"]
        Y, Z = tmps["Y"], tmps["Z"]
        if getattr(eng, "_is_gpsimd", False):
            ctr.w(eng)
            ctr.i(eng.tensor_scalar(out=Z[:, 0:F], in0=src_f32, scalar1=-1.0,
                                    scalar2=None, op0=ALU.mult))
            ctr.w(eng)
            ctr.i(eng.tensor_tensor(out=Y[:, 0:F], in0=src_f32, in1=Z[:, 0:F],
                                    op=ALU.max))
            w2 = 8
            srcv, dstv = Y, Z
            while w2 >= 1:
                ctr.w(eng)
                ctr.i(eng.tensor_tensor(
                    out=dstv[:, 0:nb * w2].rearrange("p (b k) -> p b k", k=w2),
                    in0=srcv[:, 0:nb * 2 * w2].rearrange("p (b k) -> p b k", k=2 * w2)[:, :, 0:w2],
                    in1=srcv[:, 0:nb * 2 * w2].rearrange("p (b k) -> p b k", k=2 * w2)[:, :, w2:2 * w2],
                    op=ALU.max))
                srcv, dstv = dstv, srcv
                w2 //= 2
            ctr.w(eng)
            ctr.i(eng.tensor_copy(out=amax[:, off:off + nb], in_=srcv[:, 0:nb]))
        else:
            ctr.w(eng)
            ctr.i(eng.tensor_reduce(
                out=amax[:, off:off + nb, None],
                in_=src_f32.rearrange("p (b k) -> p b k", k=16),
                axis=AX.X, op=ALU.max, apply_absolute_value=True))
        ctr.w(eng)
        ctr.i(eng.tensor_scalar(out=s6[:, off:off + nb], in0=amax[:, off:off + nb],
                                scalar1=1.0 / 6.0, scalar2=1e-30,
                                op0=ALU.mult, op1=ALU.max))

    def q_rest(eng, ctr, src_f32, out_bf, tmps, F, off):
        nb = F // 16
        s6, r = tmps["s6"], tmps["r"]
        Y, Z = tmps["Y"], tmps["Z"]
        ca, cb, sg = tmps["ca"], tmps["cb"], tmps["sg"]
        ctr.w(eng)
        ctr.i(eng.tensor_tensor(
            out=Y[:, 0:F].rearrange("p (b k) -> p b k", k=16),
            in0=src_f32.rearrange("p (b k) -> p b k", k=16),
            in1=r[:, off:off + nb, None].broadcast_to([128, nb, 16]), op=ALU.mult))
        ctr.w(eng)
        ctr.i(eng.tensor_tensor(out=Z[:, 0:F], in0=Y[:, 0:F], in1=Y[:, 0:F],
                                op=ALU.mult))
        ctr.i(eng.tensor_scalar(out=sg[:, 0:F], in0=Y[:, 0:F], scalar1=0.0,
                                scalar2=2.0, op0=ALU.is_ge, op1=ALU.mult))
        ctr.w(eng)
        ctr.i(eng.tensor_scalar(out=sg[:, 0:F], in0=sg[:, 0:F], scalar1=-1.0,
                                scalar2=None, op0=ALU.add))
        ctr.w(eng)
        ctr.i(eng.tensor_scalar(out=ca[:, 0:F], in0=Z[:, 0:F], scalar1=B2[0],
                                scalar2=QW[0], op0=ALU.is_gt, op1=ALU.mult))
        for i in range(1, 7):
            ctr.i(eng.tensor_scalar(out=cb[:, 0:F], in0=Z[:, 0:F], scalar1=B2[i],
                                    scalar2=QW[i], op0=ALU.is_gt, op1=ALU.mult))
            ctr.w(eng)
            ctr.i(eng.tensor_tensor(out=ca[:, 0:F], in0=ca[:, 0:F], in1=cb[:, 0:F],
                                    op=ALU.add))
            ctr.w(eng)
        ctr.i(eng.tensor_tensor(out=ca[:, 0:F], in0=ca[:, 0:F], in1=sg[:, 0:F],
                                op=ALU.mult))
        ctr.w(eng)
        ctr.i(eng.tensor_tensor(
            out=out_bf.rearrange("p (b k) -> p b k", k=16),
            in0=ca[:, 0:F].rearrange("p (b k) -> p b k", k=16),
            in1=s6[:, off:off + nb, None].broadcast_to([128, nb, 16]), op=ALU.mult))

    def quant_emit(eng, ctr, src_f32, out_bf, tmps, F):
        nb = F // 16
        q_amax(eng, ctr, src_f32, tmps, F, 0)
        ctr.w(eng)
        ctr.i(eng.reciprocal(out=tmps["r"][:, 0:nb], in_=tmps["s6"][:, 0:nb]))
        q_rest(eng, ctr, src_f32, out_bf, tmps, F, 0)

    def mk_qtmps(stack, prefix, F, nbw=None):
        nb = nbw if nbw is not None else F // 16
        return {
            "amax": sbuf(stack, prefix + "am", [128, nb], F32)[:],
            "s6": sbuf(stack, prefix + "s6", [128, nb], F32)[:],
            "r": sbuf(stack, prefix + "r", [128, nb], F32)[:],
            "Y": sbuf(stack, prefix + "Y", [128, F], F32)[:],
            "Z": sbuf(stack, prefix + "Z", [128, F], F32)[:],
            "ca": sbuf(stack, prefix + "ca", [128, F], BF16)[:],
            "cb": sbuf(stack, prefix + "cb", [128, F], BF16)[:],
            "sg": sbuf(stack, prefix + "sg", [128, F], BF16)[:],
        }

    def ph0(sync):
        for dst, src in [(perm2[:], perm2_in), (idb[:], idb_in), (idf[:], idf_in),
                         (gsc[:], gs_cols), (usc[:], us_cols), (dsb[:], ds_bcast),
                         (masks[:], mask_in)]:
            SDc.d(sync.dma_start(out=dst, in_=src))
    blk.sync(ph0)
    SD_CONST = SDc.n

    px = ExitStack()
    onesb = sbuf(px, "onesb", [1, S], BF16)
    xqT = sbuf(px, "xqT", [128, 16 * TOK_OWN], BF16)
    vrow = sbuf(px, "vrow", [128, DM], F32)
    vqb = sbuf(px, "vqb", [128, DM], BF16)
    grow = sbuf(px, "grow", [128, DM], F32)
    gqb = sbuf(px, "gqb", [128, DM], BF16)
    cosqf = sbuf(px, "cosqf", [128, TOK_OWN], F32)
    sinqf = sbuf(px, "sinqf", [128, TOK_OWN], F32)
    coskf = sbuf(px, "coskf", [128, S], F32)
    sinkf = sbuf(px, "sinkf", [128, S], F32)
    xt = mk_qtmps(px, "xt", 512)
    gt = mk_qtmps(px, "gt", 512, nbw=128)
    wtq = sbuf(px, "wtq", [128, 16 * 128], BF16)
    wtk = sbuf(px, "wtk", [128, 16 * 128], BF16)
    wtv = sbuf(px, "wtv", [128, 16 * 128], BF16)
    qsb = [sbuf(px, f"qsb{i}", [128, 512], BF16) for i in range(2)]
    vsb = [sbuf(px, f"vsb{i}", [128, 512], BF16) for i in range(2)]
    rro = [sbuf(px, f"rro{i}", [128, 512], BF16) for i in range(2)]
    rt1 = sbuf(px, "rt1", [128, 512], F32)
    rt2 = sbuf(px, "rt2", [128, 512], F32)
    ps_p = [psum(px, f"ps_p{i}", [128, 512]) for i in range(2)]
    ps_sh = psum(px, "ps_sh", [128, 512])

    def ph0c(sync):
        for dst, src in [(cosqf[:], cosq_in), (sinqf[:], sinq_in),
                         (coskf[:], cosk_in), (sinkf[:], sink_in),
                         (onesb[:], ones_in)]:
            SDc.d(sync.dma_start(out=dst, in_=src))
    blk.sync(ph0c)
    SD_TABS = SDc.n
    SDC_ALL = SDc.n

    xq_tile_done = {}
    vrow_free = [0]
    vqb_free = [0]

    def emit_xquant_half(half):
        for lt in range(16):
            gtile = half * 16 + lt

            def ld(sync, gtile=gtile, vfree=vrow_free[0]):
                if vfree:
                    sync.wait_ge(V.h, vfree)
                Q["x"].d(sync.dma_start(out=vrow[:],
                                        in_=x_perm[gtile * 128:(gtile + 1) * 128, :]))
            blk.sync(ld)
            ld_done = Q["x"].n

            def qn(vector, ld_done=ld_done, sfree=vqb_free[0], xqn=None):
                vector.wait_ge(Q["x"].h, ld_done)
                if sfree:
                    vector.wait_ge(Q["xq"].h, Q["xq"].n)
                for sub in range(4):
                    quant_emit(vector, V, vrow[:, sub * 512:(sub + 1) * 512],
                               vqb[:, sub * 512:(sub + 1) * 512], xt, 512)
            blk.vector(qn)
            vrow_free[0] = V.n
            q_done = V.n

            def st(scalar, gtile=gtile, q_done=q_done):
                scalar.wait_ge(V.h, q_done)
                Q["xq"].d(scalar.dma_start(out=xq_s[gtile * 128:(gtile + 1) * 128, :],
                                           in_=vqb[:]))
            blk.scalar(st)
            vqb_free[0] = Q["xq"].n
            xq_tile_done[gtile] = Q["xq"].n

    xqT_blk_done = {}

    def emit_xqT_half(half):
        free_t = T.n
        need = Q["xq"].n
        for j in range(4):
            for k in range(16):
                def tl(scalar, j=j, k=k, need=need, half=half, free_t=free_t):
                    scalar.wait_ge(Q["xq"].h, need)
                    if free_t:
                        scalar.wait_ge(T.h, free_t)
                    Q["xqT"].d(scalar.dma_start(
                        out=xqT[:, k * TOK_OWN + j * 512: k * TOK_OWN + (j + 1) * 512],
                        in_=xq_s[half * TOK_OWN + j * 512: half * TOK_OWN + (j + 1) * 512,
                                 k * 128:(k + 1) * 128],
                        transpose=True))
                blk.scalar(tl)
            xqT_blk_done[(half, j)] = Q["xqT"].n

    wq_row_done = {}
    grow_free_g = [0]
    grow_free_v = [0]
    gt_free = [0]
    gqb_free_se = [0]
    W_ENG = {"q": "v", "k": "v", "v": "g", "o": "g"}

    def emit_wtile_quant(w, p):
        def ld(sync, w=w, p=p, gfree=grow_free_g[0], vfree=grow_free_v[0]):
            if gfree:
                sync.wait_ge(G.h, gfree)
            if vfree:
                sync.wait_ge(V.h, vfree)
            Q["w"].d(sync.dma_start(out=grow[:], in_=w_in[w][p * 128:(p + 1) * 128, :]))
        blk.sync(ld)
        ld_done = Q["w"].n

        if W_ENG[w] == "v":
            def qnv(vector, ld_done=ld_done, sfree=gqb_free_se[0]):
                vector.wait_ge(Q["w"].h, ld_done)
                if sfree:
                    vector.wait_ge(Q["wst"].h, Q["wst"].n)
                for sub in range(4):
                    quant_emit(vector, V, grow[:, sub * 512:(sub + 1) * 512],
                               gqb[:, sub * 512:(sub + 1) * 512], xt, 512)
            blk.vector(qnv)
            grow_free_v[0] = V.n
            grow_free_g[0] = 0
            ctr_done, n_done = V, V.n
        else:
            def amx(vector, ld_done=ld_done, gfree=gt_free[0]):
                vector.wait_ge(Q["w"].h, ld_done)
                if gfree:
                    vector.wait_ge(G.h, gfree)
                for sub in range(4):
                    q_amax(vector, V, grow[:, sub * 512:(sub + 1) * 512], gt,
                           512, sub * 32)
                V.w(vector)
                V.i(vector.reciprocal(out=gt["r"][:, 0:128], in_=gt["s6"][:, 0:128]))
            blk.vector(amx)
            v1 = V.n

            def qng(gpsimd, v1=v1, sfree=gqb_free_se[0]):
                gpsimd._is_gpsimd = True
                gpsimd.wait_ge(V.h, v1)
                if sfree:
                    gpsimd.wait_ge(Q["wst"].h, Q["wst"].n)
                for sub in range(4):
                    q_rest(gpsimd, G, grow[:, sub * 512:(sub + 1) * 512],
                           gqb[:, sub * 512:(sub + 1) * 512], gt, 512, sub * 32)
            blk.gpsimd(qng)
            grow_free_g[0] = G.n
            grow_free_v[0] = 0
            gt_free[0] = G.n
            ctr_done, n_done = G, G.n

        def st(scalar, w=w, p=p, ctr=ctr_done, n_done=n_done):
            scalar.wait_ge(ctr.h, n_done)
            Q["wst"].d(scalar.dma_start(out=w_s[w][p * 128:(p + 1) * 128, :],
                                        in_=gqb[:]))
        blk.scalar(st)
        gqb_free_se[0] = Q["wst"].n
        wq_row_done[(w, p)] = Q["wst"].n

    wt_free = {"q": 0, "k": 0, "v": 0}

    def emit_wt_loads(w, p, dst):
        need = Q["wst"].n
        free_at = wt_free[w]
        for k in range(16):
            def tl(sync, k=k, need=need, free_at=free_at, p=p, w=w, dst=dst):
                sync.wait_ge(Q["wst"].h, need)
                if free_at:
                    sync.wait_ge(T.h, free_at)
                Q["wt"].d(sync.dma_start(out=dst[:, k * 128:(k + 1) * 128],
                                         in_=w_s[w][p * 128:(p + 1) * 128,
                                                    k * 128:(k + 1) * 128],
                                         transpose=True))
            blk.sync(tl)
        return Q["wt"].n

    ps_free = {0: 0, 1: 0}
    sh_free = [0]
    qsb_free = [0, 0]
    vsb_free = [0, 0]
    rro_free = [0, 0]

    def emit_proj(p, half, wt_ready):
        for j in range(4):
            xq_need = xqT_blk_done[(half, 3)]
            for which in (["q", "k", "v"] if half == 0 else ["k", "v"]):
                wt = {"q": wtq, "k": wtk, "v": wtv}[which]
                pi = j % 2
                psb = ps_p[pi]

                def mm(tensor, wt=wt, j=j, xq_need=xq_need, wt_ready=wt_ready,
                       psb=psb, pfree=ps_free[pi]):
                    tensor.wait_ge(Q["wt"].h, wt_ready)
                    tensor.wait_ge(Q["xqT"].h, xq_need)
                    if pfree:
                        tensor.wait_ge(V.h, pfree)
                    li = None
                    for k in range(16):
                        li = tensor.matmul(
                            psb[:], wt[:, k * 128:(k + 1) * 128],
                            xqT[:, k * TOK_OWN + j * 512: k * TOK_OWN + j * 512 + 512],
                            start=(k == 0), stop=(k == 15))
                    T.i(li)
                blk.tensor(mm)
                mm_done = T.n

                if which == "v":
                    pb = j % 2

                    def ev(vector, psb=psb, mm_done=mm_done, pb=pb,
                           sfree=vsb_free[pb]):
                        vector.wait_ge(T.h, mm_done)
                        if sfree:
                            vector.wait_ge(Q["vqk"].h, Q["vqk"].n)
                        V.i(vector.tensor_copy(out=vsb[pb][:], in_=psb[:]))
                    blk.vector(ev)
                    ps_free[pi] = V.n
                    ev_done = V.n
                    col0 = half * TOK_OWN + j * 512

                    def st(scalar, p=p, col0=col0, pb=pb, ev_done=ev_done):
                        scalar.wait_ge(V.h, ev_done)
                        Q["vqk"].d(scalar.dma_start(out=v_s[p, 0:64, col0:col0 + 512],
                                                    in_=vsb[pb][0:64, :]))
                        Q["vqk"].d(scalar.dma_start(out=v_s[p, 80:144, col0:col0 + 512],
                                                    in_=vsb[pb][64:128, :]))
                    blk.scalar(st)
                    vsb_free[pb] = Q["vqk"].n
                else:
                    ct = cosqf if which == "q" else coskf
                    st_t = sinqf if which == "q" else sinkf
                    tab0 = j * 512 if which == "q" else half * TOK_OWN + j * 512
                    pb = j % 2

                    def ev1(vector, psb=psb, mm_done=mm_done, pb=pb,
                            qfree=qsb_free[pb]):
                        vector.wait_ge(T.h, mm_done)
                        if qfree:
                            vector.wait_ge(T.h, qfree)
                        V.i(vector.tensor_copy(out=qsb[pb][:], in_=psb[:]))
                    blk.vector(ev1)
                    ev1_done = V.n

                    def shmm(tensor, pb=pb, ev1_done=ev1_done, sfree=sh_free[0]):
                        tensor.wait_ge(SDc.h, SDC_ALL)
                        tensor.wait_ge(V.h, ev1_done)
                        if sfree:
                            tensor.wait_ge(V.h, sfree)
                        T.i(tensor.matmul(ps_sh[:], perm2[:], qsb[pb][:],
                                          start=True, stop=True))
                    blk.tensor(shmm)
                    sh_done = T.n
                    qsb_free[pb] = sh_done

                    def ev2(vector, psb=psb, sh_done=sh_done, pb=pb, ct=ct,
                            st_t=st_t, tab0=tab0, rfree=rro_free[pb]):
                        vector.wait_ge(T.h, sh_done)
                        vector.wait_ge(SDc.h, SDC_ALL)
                        V.i(vector.tensor_tensor(out=rt1[:], in0=psb[:],
                                                 in1=ct[:, tab0:tab0 + 512],
                                                 op=ALU.mult))
                        V.i(vector.tensor_tensor(out=rt2[:], in0=ps_sh[:],
                                                 in1=st_t[:, tab0:tab0 + 512],
                                                 op=ALU.mult))
                        V.w(vector)
                        if rfree:
                            vector.wait_ge(Q["vqk"].h, Q["vqk"].n)
                        V.i(vector.tensor_tensor(out=rro[pb][:], in0=rt1[:],
                                                 in1=rt2[:], op=ALU.add))
                    blk.vector(ev2)
                    ev2_done = V.n
                    ps_free[pi] = ev2_done
                    sh_free[0] = ev2_done

                    if which == "q":
                        dst = q_s[p, :, j * 512:(j + 1) * 512]
                    else:
                        dst = k_s[p, :, half * TOK_OWN + j * 512:
                                  half * TOK_OWN + (j + 1) * 512]

                    def st(scalar, dst=dst, pb=pb, ev2_done=ev2_done):
                        scalar.wait_ge(V.h, ev2_done)
                        Q["vqk"].d(scalar.dma_start(out=dst, in_=rro[pb][:]))
                    blk.scalar(st)
                    rro_free[pb] = Q["vqk"].n

    emit_xquant_half(0)
    emit_xqT_half(0)
    for p in range(PAIRS):
        for w in ("q", "k", "v"):
            emit_wtile_quant(w, p)
        rq = emit_wt_loads("q", p, wtq)
        rk = emit_wt_loads("k", p, wtk)
        rv = emit_wt_loads("v", p, wtv)
        emit_proj(p, 0, max(rq, rk, rv))
        wt_free["q"] = wt_free["k"] = wt_free["v"] = T.n
    emit_xquant_half(1)
    emit_xqT_half(1)
    for p in range(PAIRS):
        rk = emit_wt_loads("k", p, wtk)
        rv = emit_wt_loads("v", p, wtv)
        emit_proj(p, 1, max(rk, rv))
        wt_free["k"] = wt_free["v"] = T.n
    for p in range(PAIRS):
        emit_wtile_quant("o", p)

    def ones_st(scalar, need=None):
        scalar.wait_ge(SDc.h, SDc.n)
        for p in range(PAIRS):
            Q["vqk"].d(scalar.dma_start(out=v_s[p, 64:80, :],
                                        in_=ones_in.broadcast_to([16, S])))
            Q["vqk"].d(scalar.dma_start(out=v_s[p, 144:160, :],
                                        in_=ones_in.broadcast_to([16, S])))
    blk.scalar(ones_st)
    SCRATCH_QKV = Q["vqk"].n
    WO_ROWS = {p: wq_row_done[("o", p)] for p in range(PAIRS)}
    PX_T, PX_G, PX_V = T.n, G.n, V.n
    px.close()

    pao = ExitStack()
    oq_res = [sbuf(pao, f"oq{t}", [128, DM], BF16) for t in range(16)]
    woT = sbuf(pao, "woT", [128, 16 * DM], BF16)
    cbuf = sbuf(pao, "cbuf", [128, 512], F32)
    cout = sbuf(pao, "cout", [128, 512], BF16)
    pa = ExitStack()
    kpair = [sbuf(pa, f"kpair{i}", [128, S], BF16) for i in range(2)]
    qpair = [sbuf(pa, f"qpair{i}", [128, TOK_OWN], BF16) for i in range(2)]
    vab = [sbuf(pa, f"vab{i}", [128, 32 * 160], BF16) for i in range(2)]
    pband = [sbuf(pa, f"pband{i}", [128, 1024], BF16) for i in range(2)]
    dnm = sbuf(pa, "dnm", [128, 4], F32)
    rc4 = sbuf(pa, "rc4", [128, 4], F32)
    oqt = mk_qtmps(pa, "oqt", 64)
    st_ps = [psum(pa, f"st_ps{i}", [128, 1024]) for i in range(2)]
    otr_ps = psum(pa, "otr_ps", [128, 260])
    otr_ps2 = psum(pa, "otr_ps2", [128, 260])
    otr_bank = [otr_ps, otr_ps2]
    osum = sbuf(pa, "osum", [128, 260], F32)


    cast_jobs = []
    for src_t, dst_t, rows in [(gwT_in, gwb_s, DM), (uwT_in, uwb_s, DM),
                               (dwT_in, dwb_s, INTER)]:
        for rt in range(rows // 128):
            for ct2 in range({DM: INTER, INTER: DM}[rows] // 512):
                cast_jobs.append((src_t, dst_t, rt, ct2))
    cast_free = [0]
    cast_st_free = [0]

    def emit_cast_slice(jobs):
        for src_t, dst_t, rt, ct2 in jobs:
            def ld(sync, src_t=src_t, rt=rt, ct2=ct2, cfree=cast_free[0]):
                sync.wait_ge(T.h, PX_T)
                sync.wait_ge(V.h, PX_V)
                sync.wait_ge(G.h, max(PX_G, cfree))
                Q["cast"].d(sync.dma_start(
                    out=cbuf[:], in_=src_t[rt * 128:(rt + 1) * 128,
                                           ct2 * 512:(ct2 + 1) * 512]))
            blk.sync(ld)
            ld_done = Q["cast"].n

            def cs(gpsimd, dst_t=dst_t, rt=rt, ct2=ct2, ld_done=ld_done,
                   stfree=cast_st_free[0]):
                gpsimd.wait_ge(Q["cast"].h, ld_done)
                if stfree:
                    gpsimd.wait_ge(SGQ.h, stfree)
                G.i(gpsimd.tensor_copy(out=cout[:], in_=cbuf[:]))
                G.w(gpsimd)
                SGQ.d(gpsimd.dma_start(out=dst_t[rt * 128:(rt + 1) * 128,
                                                 ct2 * 512:(ct2 + 1) * 512],
                                       in_=cout[:]))
            blk.gpsimd(cs)
            cast_free[0] = G.n
            cast_st_free[0] = SGQ.n

    pair_load_done = {}
    pair_buf_free = [0, 0]

    def emit_pair_loads(p):
        bi = p % 2

        def ld(sync, p=p, bi=bi, freed=pair_buf_free[bi]):
            sync.wait_ge(Q["vqk"].h, SCRATCH_QKV)
            sync.wait_ge(T.h, max(PX_T, freed))
            sync.wait_ge(G.h, PX_G)
            sync.wait_ge(V.h, PX_V)
            SDp[bi].d(sync.dma_start(out=kpair[bi][:], in_=k_s[p]))
            SDp[bi].d(sync.dma_start(out=qpair[bi][:], in_=q_s[p]))
            for t in range(32):
                SDp[bi].d(sync.dma_start(out=vab[bi][:, t * 160:(t + 1) * 160],
                                         in_=v_s[p, :, t * 128:(t + 1) * 128],
                                         transpose=True))
        blk.sync(ld)
        pair_load_done[p] = SDp[bi].n

    def kv_list(g):
        i = 2 * g
        out = []
        for j in range(i + 2):
            out.append((j, 0 if j == i else (1 if j == i + 1 else None)))
        for j in range(i + 2):
            out.append((16 + j, 2 if j == i else (3 if j == i + 1 else None)))
        return out

    stps_free = [0, 0]
    pband_free = [0, 0]
    otr_free = [0]
    obank_free = [0, 0]

    def emit_pair_attn(p):
        bi = p % 2
        ld_need = pair_load_done[p]
        for g in range(8):
            q0 = g * 256
            tiles = kv_list(g)
            ngr = len(tiles) // 2
            for gr in range(ngr):
                pbuf = gr % 2
                t0, m0 = tiles[2 * gr]
                t1, m1 = tiles[2 * gr + 1]
                ps = st_ps[pbuf]

                def stmm(tensor, bi=bi, ld_need=ld_need, q0=q0, t0=t0, t1=t1,
                         ps=ps, free_sc=stps_free[pbuf]):
                    tensor.wait_ge(SDp[bi].h, ld_need)
                    if free_sc:
                        tensor.wait_ge(SC.h, free_sc)
                    li = None
                    for sl, tt in ((0, t0), (1, t1)):
                        tensor.matmul(ps[:, sl * 256:sl * 256 + 256],
                                      kpair[bi][0:64, tt * 128:(tt + 1) * 128],
                                      qpair[bi][0:64, q0:q0 + 256],
                                      start=True, stop=True)
                        li = tensor.matmul(ps[:, 512 + sl * 256:512 + sl * 256 + 256],
                                           kpair[bi][64:128, tt * 128:(tt + 1) * 128],
                                           qpair[bi][64:128, q0:q0 + 256],
                                           start=True, stop=True)
                    T.i(li)
                blk.tensor(stmm)
                st_done = T.n

                mask_done = None
                if m0 is not None or m1 is not None:
                    def mk(vector, ps=ps, m0=m0, m1=m1, st_done=st_done):
                        vector.wait_ge(T.h, st_done)
                        vector.wait_ge(SDc.h, SDC_ALL)
                        for sl, m in ((0, m0), (1, m1)):
                            if m is None:
                                continue
                            V.i(vector.tensor_tensor(
                                out=ps[:, sl * 256:(sl + 1) * 256],
                                in0=ps[:, sl * 256:(sl + 1) * 256],
                                in1=masks[:, m * 512:m * 512 + 256], op=ALU.add))
                            V.i(vector.tensor_tensor(
                                out=ps[:, 512 + sl * 256:512 + (sl + 1) * 256],
                                in0=ps[:, 512 + sl * 256:512 + (sl + 1) * 256],
                                in1=masks[:, m * 512 + 256:(m + 1) * 512],
                                op=ALU.add))
                    blk.vector(mk)
                    mask_done = V.n

                def ex(scalar, ps=ps, pbuf=pbuf, st_done=st_done,
                       mask_done=mask_done, pb_free=pband_free[pbuf]):
                    if mask_done is not None:
                        scalar.wait_ge(V.h, mask_done)
                    else:
                        scalar.wait_ge(T.h, st_done)
                    if pb_free:
                        scalar.wait_ge(T.h, pb_free)
                    SC.i(scalar.activation(out=pband[pbuf][:], in_=ps[:],
                                           func=AF.Exp))
                blk.scalar(ex)
                exp_done = SC.n
                stps_free[pbuf] = exp_done

                def avmm(tensor, bi=bi, pbuf=pbuf, exp_done=exp_done,
                         t0=t0, t1=t1, gr=gr,
                         ofree=obank_free[gr % 2]):
                    tensor.wait_ge(SC.h, exp_done)
                    if ofree:
                        tensor.wait_ge(V.h, ofree)
                    li = None
                    ps_o = otr_bank[gr % 2]
                    for sl, tt in ((0, t0), (1, t1)):
                        for qc in range(2):
                            for h, voff in ((0, 0), (1, 80)):
                                first = (sl == 0 and qc == 0 and h == 0)
                                pcol = (sl * 256 if h == 0 else 512 + sl * 256) + qc * 128
                                w0 = (h * 2 + qc) * 65
                                li = tensor.matmul(
                                    ps_o[:, w0:w0 + 65],
                                    pband[pbuf][:, pcol:pcol + 128],
                                    vab[bi][:, tt * 160 + voff:tt * 160 + voff + 65],
                                    start=first, stop=(sl == 1),
                                    skip_group_check=True)
                    T.i(li)
                blk.tensor(avmm)
                pband_free[pbuf] = T.n
                av_done = T.n

                def oadd(vector, gr=gr, av_done=av_done):
                    vector.wait_ge(T.h, av_done)
                    V.w(vector)
                    ps_o = otr_bank[gr % 2]
                    if gr == 0:
                        V.i(vector.tensor_copy(out=osum[:], in_=ps_o[:]))
                    else:
                        V.i(vector.tensor_tensor(out=osum[:], in0=osum[:],
                                                 in1=ps_o[:], op=ALU.add))
                blk.vector(oadd)
                obank_free[gr % 2] = V.n

            def oquant(vector, p=p, g=g):
                V.w(vector)
                V.i(vector.tensor_copy(out=dnm[:], in_=osum[:, 64:260:65]))
                V.w(vector)
                V.i(vector.reciprocal(out=rc4[:], in_=dnm[:]))
                V.w(vector)
                for t in range(4):
                    head = 2 * p + (0 if t < 2 else 1)
                    qt = 2 * g + (t % 2)
                    quant_emit(vector, V, osum[:, t * 65:t * 65 + 64],
                               oq_res[qt][:, head * 64:(head + 1) * 64], oqt, 64)
                    V.w(vector)
                    V.i(vector.tensor_scalar(
                        out=oq_res[qt][:, head * 64:(head + 1) * 64],
                        in0=oq_res[qt][:, head * 64:(head + 1) * 64],
                        scalar1=rc4[:, t:t + 1], scalar2=None, op0=ALU.mult))
            blk.vector(oquant)
            otr_free[0] = V.n
        pair_buf_free[bi] = T.n

    ncast = len(cast_jobs)
    cast_per_pair = (ncast + PAIRS - 1) // PAIRS
    emit_pair_loads(0)
    emit_pair_loads(1)
    woT_chunk_done = {}
    WO_TL_DONE = [0]
    for p in range(PAIRS):
        emit_pair_attn(p)
        if p + 2 < PAIRS:
            emit_pair_loads(p + 2)
        emit_cast_slice(cast_jobs[p * cast_per_pair:(p + 1) * cast_per_pair])

    def wo_tl(scalar, need=None):
        scalar.wait_ge(Q["wst"].h, Q["wst"].n)
        scalar.wait_ge(T.h, PX_T)
        scalar.wait_ge(V.h, PX_V)
        for k in range(16):
            for s2 in range(4):
                Q["woT"].d(scalar.dma_start(
                    out=woT[:, k * DM + s2 * 512: k * DM + (s2 + 1) * 512],
                    in_=w_s["o"][s2 * 512:(s2 + 1) * 512,
                                 k * 128:(k + 1) * 128],
                    transpose=True))
    blk.scalar(wo_tl)
    WO_TL_DONE[0] = Q["woT"].n
    ATTN_V = V.n
    ATTN_T = T.n
    ATTN_SC = SC.n
    CAST_DONE = SGQ.n
    pa.close()

    pw = ExitStack()
    oqT = sbuf(pw, "oqT", [128, 16 * 128], BF16)
    xown = [sbuf(pw, f"xown{i}", [128, DM], F32) for i in range(2)]
    h1sb = sbuf(pw, "h1sb", [128, DM], F32)
    h1nb = [sbuf(pw, f"h1nb{i}", [128, DM], BF16) for i in range(2)]
    h1nf = [sbuf(pw, f"h1nf{i}", [128, DM], F32) for i in range(2)]
    lnt = {k2: sbuf(pw, "ln_" + k2, [128, 1], F32)
           for k2 in ["sum", "ssq", "nm", "var", "m2", "rstd", "rs2"]}
    ps_h1 = [psum(pw, f"ps_h1{i}", [128, 512]) for i in range(4)]
    ps_tr = psum(pw, "ps_tr", [128, 1024], BF16)

    def emit_ln(src_ap, out_bf_ap, out_f32_ap, lntd, scrd):
        def v1(vector):
            V.i(vector.tensor_reduce(out=lntd["sum"][:, :, None],
                                     in_=src_ap.unsqueeze(1), axis=AX.X, op=ALU.add))
            V.i(vector.scalar_tensor_tensor(out=scrd, in0=src_ap, scalar=0.0,
                                            in1=src_ap, op0=ALU.bypass,
                                            op1=ALU.mult,
                                            accum_out=lntd["ssq"][:]))
            V.w(vector)
            V.i(vector.tensor_scalar(out=lntd["nm"][:], in0=lntd["sum"][:],
                                     scalar1=-1.0 / DM, scalar2=None, op0=ALU.mult))
            V.w(vector)
            V.i(vector.tensor_tensor(out=lntd["m2"][:], in0=lntd["nm"][:],
                                     in1=lntd["nm"][:], op=ALU.mult))
            V.i(vector.tensor_scalar(out=lntd["var"][:], in0=lntd["ssq"][:],
                                     scalar1=1.0 / DM, scalar2=None, op0=ALU.mult))
            V.w(vector)
            V.i(vector.tensor_tensor(out=lntd["var"][:], in0=lntd["var"][:],
                                     in1=lntd["m2"][:], op=ALU.subtract))
        blk.vector(v1)
        nv = V.n

        def s1(scalar, nv=nv):
            scalar.wait_ge(V.h, nv)
            SC.i(scalar.activation(out=lntd["rs2"][:], in_=lntd["var"][:],
                                   func=AF.Sqrt, bias=epsc[:], scale=1.0))
        blk.scalar(s1)
        ns = SC.n

        def v2(vector, ns=ns):
            vector.wait_ge(SC.h, ns)
            V.i(vector.reciprocal(out=lntd["rstd"][:], in_=lntd["rs2"][:]))
            V.w(vector)
            if out_bf_ap is not None:
                V.i(vector.tensor_scalar(out=out_bf_ap, in0=src_ap,
                                         scalar1=lntd["nm"][:],
                                         scalar2=lntd["rstd"][:],
                                         op0=ALU.add, op1=ALU.mult))
            V.i(vector.tensor_scalar(out=out_f32_ap, in0=src_ap,
                                     scalar1=lntd["nm"][:],
                                     scalar2=lntd["rstd"][:],
                                     op0=ALU.add, op1=ALU.mult))
        blk.vector(v2)

    oqT_free = [0]
    psh1_free = [0]
    xown_free = [0, 0]
    h1buf_free = [0, 0]

    for t in range(16):
        bi = t % 2
        for half8 in range(2):
            def tr(tensor, t=t, half8=half8, ofree=oqT_free[0]):
                tensor.wait_ge(SDc.h, SDC_ALL)
                tensor.wait_ge(V.h, ATTN_V)
                if ofree:
                    tensor.wait_ge(V.h, ofree)
                li = None
                for kk in range(8):
                    k = half8 * 8 + kk
                    li = tensor.transpose(ps_tr[:, kk * 128:(kk + 1) * 128],
                                          oq_res[t][:, k * 128:(k + 1) * 128],
                                          idb[:])
                T.i(li)
            blk.tensor(tr)
            ntr = T.n

            def cp(vector, half8=half8, ntr=ntr):
                vector.wait_ge(T.h, ntr)
                V.i(vector.tensor_copy(out=oqT[:, half8 * 1024:(half8 + 1) * 1024],
                                       in_=ps_tr[:]))
            blk.vector(cp)
            oqT_free[0] = V.n
        cp_done = V.n

        def mm(tensor, cp_done=cp_done, pfree=psh1_free[0]):
            tensor.wait_ge(V.h, cp_done)
            if pfree:
                tensor.wait_ge(V.h, pfree)
            tensor.wait_ge(Q["woT"].h, WO_TL_DONE[0])
            li = None
            for k in range(16):
                for s2 in range(4):
                    li = tensor.matmul(ps_h1[s2][:], oqT[:, k * 128:(k + 1) * 128],
                                       woT[:, k * DM + s2 * 512: k * DM + (s2 + 1) * 512],
                                       start=(k == 0), stop=(k == 15))
            T.i(li)
        blk.tensor(mm)
        mm_done = T.n

        def xl(sync, t=t, bi=bi, xfree=xown_free[bi]):
            sync.wait_ge(T.h, ATTN_T)
            sync.wait_ge(SC.h, ATTN_SC)
            if xfree:
                sync.wait_ge(V.h, xfree)
            Q[f"xl{bi}"].d(sync.dma_start(out=xown[bi][:],
                                          in_=x_perm[t * 128:(t + 1) * 128, :]))
        blk.sync(xl)
        x_done = Q[f"xl{bi}"].n

        def ev(vector, bi=bi, mm_done=mm_done, x_done=x_done,
               hfree=h1buf_free[bi]):
            vector.wait_ge(T.h, mm_done)
            vector.wait_ge(Q[f"xl{bi}"].h, x_done)
            if hfree:
                vector.wait_ge(Q["h1"].h, Q["h1"].n)
            for s2 in range(4):
                V.i(vector.tensor_tensor(out=h1sb[:, s2 * 512:(s2 + 1) * 512],
                                         in0=ps_h1[s2][:],
                                         in1=xown[bi][:, s2 * 512:(s2 + 1) * 512],
                                         op=ALU.add))
            V.w(vector)
        blk.vector(ev)
        psh1_free[0] = V.n
        xown_free[bi] = V.n
        emit_ln(h1sb[:], h1nb[bi][:], h1nf[bi][:], lnt, h1nf[bi][:])
        ln_done = V.n

        def st(scalar, t=t, bi=bi, ln_done=ln_done):
            scalar.wait_ge(V.h, ln_done)
            Q["h1"].d(scalar.dma_start(out=h1n_s[t * 128:(t + 1) * 128, :],
                                       in_=h1nf[bi][:]))
            Q["h1"].d(scalar.dma_start(out=h1nb_s[t * 128:(t + 1) * 128, :],
                                       in_=h1nb[bi][:]))
        blk.scalar(st)
        h1buf_free[bi] = Q["h1"].n
        oqT_free[0] = V.n
    H1_DONE = Q["h1"].n
    WO_T = T.n
    WO_V = V.n
    WO_SC = SC.n
    pw.close()
    pao.close()

    pf = ExitStack()
    hT = sbuf(pf, "hT", [128, NIT * 512], BF16)
    acc = [sbuf(pf, f"acc{i}", [128, DM], F32) for i in range(4)]
    h1nT = sbuf(pf, "h1nT", [128, 16 * 512], BF16)
    gw_it = [sbuf(pf, f"gw_it{i}", [128, 16 * 128], BF16) for i in range(2)]
    uw_it = [sbuf(pf, f"uw_it{i}", [128, 16 * 128], BF16) for i in range(2)]
    gsb = sbuf(pf, "gsb", [128, 512], F32)
    slb = sbuf(pf, "slb", [128, 512], F32)
    usb = sbuf(pf, "usb", [128, 512], F32)
    dwc = [sbuf(pf, f"dwc{i}", [128, 512], BF16) for i in range(2)]
    ysb = [sbuf(pf, f"ysb{i}", [128, DM], F32) for i in range(2)]
    lnt2 = {k2: sbuf(pf, "l2_" + k2, [128, 1], F32)
            for k2 in ["sum", "ssq", "nm", "var", "m2", "rstd", "rs2"]}
    scr2k2 = sbuf(pf, "scr2k2", [128, DM], F32)
    ps_g = [psum(pf, f"ps_g{i}", [128, 512]) for i in range(2)]
    ps_u = [psum(pf, f"ps_u{i}", [128, 512]) for i in range(2)]
    ps_d = [psum(pf, f"ps_d{i}", [128, 512]) for i in range(4)]

    h1nT_free = [0]
    gw_free = [0, 0]
    psgu_free = [0, 0]
    hT_free = [0]
    dwc_free = [0, 0]
    psd_free = [0, 0, 0, 0]
    acc_ln_done = [0, 0, 0, 0]
    ysb_free = [0, 0]

    for sb_i in range(4):
        tok0 = sb_i * 512

        def tl2(scalar, tok0=tok0, hfree=h1nT_free[0], need=None):
            scalar.wait_ge(Q["h1"].h, H1_DONE)
            scalar.wait_ge(SGQ.h, CAST_DONE)
            scalar.wait_ge(T.h, max(WO_T, hfree))
            scalar.wait_ge(V.h, WO_V)
            scalar.wait_ge(SC.h, WO_SC)
            for k in range(16):
                Q["woT"].d(scalar.dma_start(
                    out=h1nT[:, k * 512:(k + 1) * 512],
                    in_=h1nb_s[tok0:tok0 + 512, k * 128:(k + 1) * 128],
                    transpose=True))
        blk.scalar(tl2)
        h1nT_done = Q["woT"].n

        for it in range(NIT):
            bi = it % 2

            def wld(sync, it=it, bi=bi, gfree=gw_free[bi]):
                sync.wait_ge(SGQ.h, CAST_DONE)
                sync.wait_ge(T.h, max(WO_T, gfree))
                sync.wait_ge(V.h, WO_V)
                Q[f"ffw{bi}"].d(sync.dma_start(
                    out=gw_it[bi][:].rearrange("p (k c) -> p k c", c=128),
                    in_=gwb_s.rearrange("(k p) i -> p k i", p=128)[:, :, it * 128:(it + 1) * 128]))
                Q[f"ffw{bi}"].d(sync.dma_start(
                    out=uw_it[bi][:].rearrange("p (k c) -> p k c", c=128),
                    in_=uwb_s.rearrange("(k p) i -> p k i", p=128)[:, :, it * 128:(it + 1) * 128]))
            blk.sync(wld)
            w_done = Q[f"ffw{bi}"].n

            def gumm(tensor, bi=bi, w_done=w_done, h1nT_done=h1nT_done,
                     pfree=psgu_free[bi]):
                tensor.wait_ge(Q[f"ffw{bi}"].h, w_done)
                tensor.wait_ge(Q["woT"].h, h1nT_done)
                if pfree:
                    tensor.wait_ge(V.h, pfree)
                li = None
                for k in range(16):
                    tensor.matmul(ps_g[bi][:], gw_it[bi][:, k * 128:(k + 1) * 128],
                                  h1nT[:, k * 512:(k + 1) * 512],
                                  start=(k == 0), stop=(k == 15))
                for k in range(16):
                    li = tensor.matmul(ps_u[bi][:], uw_it[bi][:, k * 128:(k + 1) * 128],
                                       h1nT[:, k * 512:(k + 1) * 512],
                                       start=(k == 0), stop=(k == 15))
                T.i(li)
            blk.tensor(gumm)
            gu_done = T.n
            gw_free[bi] = gu_done

            def gl1(vector, it=it, bi=bi, gu_done=gu_done, sc_prev=SC.n):
                vector.wait_ge(T.h, gu_done)
                vector.wait_ge(SDc.h, SDC_ALL)
                V.w(vector)
                vector.wait_ge(SC.h, sc_prev)
                V.i(vector.tensor_scalar(out=gsb[:], in0=ps_g[bi][:],
                                         scalar1=gsc[:, it:it + 1], scalar2=None,
                                         op0=ALU.mult))
                V.i(vector.tensor_scalar(out=usb[:], in0=ps_u[bi][:],
                                         scalar1=usc[:, it:it + 1], scalar2=None,
                                         op0=ALU.mult))
            blk.vector(gl1)
            g1_done = V.n

            def sil(scalar, g1_done=g1_done):
                scalar.wait_ge(V.h, g1_done)
                SC.i(scalar.activation(out=slb[:], in_=gsb[:], func=AF.Silu))
            blk.scalar(sil)
            sil_done = SC.n

            def gl2(vector, it=it, sil_done=sil_done, hfree=hT_free[0]):
                vector.wait_ge(SC.h, sil_done)
                if hfree and it == 0:
                    vector.wait_ge(T.h, hfree)
                V.i(vector.tensor_tensor(out=hT[:, it * 512:(it + 1) * 512],
                                         in0=slb[:], in1=usb[:], op=ALU.mult))
            blk.vector(gl2)
            psgu_free[bi] = V.n
        h_done = V.n

        for tt in range(4):
            gtile = sb_i * 4 + tt

            def h1ld(sync, gtile=gtile, tt=tt, lfree=acc_ln_done[tt], cur=None):
                sync.wait_ge(Q["h1"].h, H1_DONE)
                sync.wait_ge(SGQ.h, CAST_DONE)
                sync.wait_ge(T.h, WO_T)
                sync.wait_ge(V.h, max(WO_V, lfree))
                sync.wait_ge(SC.h, WO_SC)
                SDacc.d(sync.dma_start(out=acc[tt][:],
                                       in_=h1n_s[gtile * 128:(gtile + 1) * 128, :]))
            blk.sync(h1ld)
        acc_ld = SDacc.n

        for s2 in range(4):
            for k in range(NIT):
                kb = k % 2

                def dwld(sync, s2=s2, k=k, kb=kb, dfree=dwc_free[kb]):
                    sync.wait_ge(SGQ.h, CAST_DONE)
                    sync.wait_ge(T.h, max(WO_T, dfree))
                    sync.wait_ge(V.h, WO_V)
                    Q[f"dw{kb}"].d(sync.dma_start(out=dwc[kb][:],
                                                  in_=dwb_s[k * 128:(k + 1) * 128,
                                                            s2 * 512:(s2 + 1) * 512]))
                blk.sync(dwld)
                dw_done = Q[f"dw{kb}"].n

                def dmm(tensor, s2=s2, k=k, kb=kb, dw_done=dw_done,
                        h_done=h_done, pfmx=max(psd_free)):
                    tensor.wait_ge(Q[f"dw{kb}"].h, dw_done)
                    if k == 0:
                        tensor.wait_ge(V.h, h_done)
                        if pfmx:
                            tensor.wait_ge(V.h, pfmx)
                    li = None
                    for tt in range(4):
                        li = tensor.matmul(
                            ps_d[tt][:],
                            hT[:, k * 512 + tt * 128: k * 512 + (tt + 1) * 128],
                            dwc[kb][:], start=(k == 0), stop=(k == NIT - 1))
                    T.i(li)
                blk.tensor(dmm)
                dwc_free[kb] = T.n
            dmm_done = T.n

            def dev(vector, s2=s2, dmm_done=dmm_done, acc_ld=acc_ld):
                vector.wait_ge(SDc.h, SDC_ALL)
                vector.wait_ge(T.h, dmm_done)
                vector.wait_ge(SDacc.h, acc_ld)
                for tt in range(4):
                    V.i(vector.scalar_tensor_tensor(
                        out=scr2k2[:, 0:512], in0=ps_d[tt][:], scalar=1.0,
                        in1=dsb[:, s2 * 512:(s2 + 1) * 512],
                        op0=ALU.mult, op1=ALU.mult))
                    V.w(vector)
                    V.i(vector.tensor_tensor(out=acc[tt][:, s2 * 512:(s2 + 1) * 512],
                                             in0=acc[tt][:, s2 * 512:(s2 + 1) * 512],
                                             in1=scr2k2[:, 0:512], op=ALU.add))
                    V.w(vector)
            blk.vector(dev)
            for tt in range(4):
                psd_free[tt] = V.n
        hT_free[0] = T.n
        h1nT_free[0] = T.n

        for tt in range(4):
            gtile = sb_i * 4 + tt
            bi = tt % 2

            def pre(vector, bi=bi, yfree=ysb_free[bi]):
                if yfree:
                    vector.wait_ge(Q[f"y{bi}"].h, yfree)
                V.i(vector.engine_nop())
            blk.vector(pre)
            emit_ln(acc[tt][:], None, ysb[bi][:], lnt2, scr2k2[:])
            ln_done = V.n

            def yst(sync, gtile=gtile, bi=bi, ln_done=ln_done):
                sync.wait_ge(V.h, ln_done)
                Q[f"y{bi}"].d(sync.dma_start(out=y_out[gtile * 128:(gtile + 1) * 128, :],
                                             in_=ysb[bi][:]))
            blk.sync(yst)
            ysb_free[bi] = Q[f"y{bi}"].n
            acc_ln_done[tt] = ln_done

    def fin(sync):
        for qn2 in _QNAMES:
            if Q[qn2].n:
                sync.wait_ge(Q[qn2].h, Q[qn2].n)
    blk.sync(fin)

    pf.close()
    es.close()
    return nc



def _host_consts(hh):
    f32 = np.float32
    HD = 64
    inv_freq = (1.0 / (f32(10000.0) ** (np.arange(0, HD, 2, dtype=f32) / f32(HD)))).astype(f32)
    pos = np.arange(S, dtype=f32)
    freqs = pos[:, None] * inv_freq[None, :]
    cos = np.cos(freqs).astype(f32)
    sin = np.sin(freqs).astype(f32)
    own = [hh + 2 * i for i in range(16)]
    oth = [(1 - hh) + 2 * i for i in range(16)]
    perm_tiles = own + oth
    tok_perm = np.concatenate([np.arange(t * 128, (t + 1) * 128) for t in perm_tiles])
    cosP, sinP = cos[tok_perm], sin[tok_perm]

    def tab(c, s2, n, scale):
        ct = np.zeros((128, n), dtype=f32)
        st = np.zeros((128, n), dtype=f32)
        for d in range(64):
            ct[d] = ct[64 + d] = c[:n, d % 32] * scale
            sgn = -1.0 if d < 32 else 1.0
            st[d] = st[64 + d] = sgn * s2[:n, d % 32] * scale
        return ct, st
    cosq, sinq = tab(cosP, sinP, TOK_OWN, f32(0.125))
    cosk, sink = tab(cosP, sinP, S, f32(1.0))

    tri = np.where(np.arange(128)[:, None] <= np.arange(128)[None, :], 0.0, NEG).astype(f32)
    zer = np.zeros((128, 128), dtype=f32)
    neg = np.full((128, 128), NEG, dtype=f32)

    def mk(b0, b1):
        m = np.concatenate([b0, b1], axis=1)
        return np.concatenate([m, m], axis=1).astype(f32)
    m_oa = mk(tri, zer)
    m_ob = mk(neg, tri)
    if hh == 0:
        m_ua, m_ub = mk(neg, zer), mk(neg, neg)
    else:
        m_ua, m_ub = mk(zer, zer), mk(neg, zer)
    mask = np.concatenate([m_oa, m_ob, m_ua, m_ub], axis=1)

    p2 = np.zeros((128, 128), dtype=np.float32)
    for p in range(128):
        r = (p // 64) * 64 + ((p % 64) + 32) % 64
        p2[r, p] = 1.0
    return dict(perm_tiles=perm_tiles, tok_perm=tok_perm,
                cosq_in=cosq, sinq_in=sinq, cosk_in=cosk, sink_in=sink,
                mask_in=mask,
                perm2_in=p2.astype(ml_dtypes.bfloat16),
                idb_in=np.eye(128, dtype=ml_dtypes.bfloat16),
                idf_in=np.eye(128, dtype=np.float32),
                ones_in=np.ones((1, S), dtype=ml_dtypes.bfloat16))


def kernel(L0_attn_hidden_states, L0_attn_Wq, L0_attn_Wk, L0_attn_Wv, L0_attn_Wo,
           L0_ffn_gate_w, L0_ffn_up_w, L0_ffn_down_w,
           L0_ffn_gate_s, L0_ffn_up_s, L0_ffn_down_s):
    f32 = np.float32
    x = np.asarray(L0_attn_hidden_states, dtype=f32)
    wq = np.ascontiguousarray(np.asarray(L0_attn_Wq, dtype=f32))
    wk = np.ascontiguousarray(np.asarray(L0_attn_Wk, dtype=f32))
    wv = np.ascontiguousarray(np.asarray(L0_attn_Wv, dtype=f32))
    wo = np.ascontiguousarray(np.asarray(L0_attn_Wo, dtype=f32))
    gwT = np.ascontiguousarray(np.asarray(L0_ffn_gate_w, dtype=f32).T)
    uwT = np.ascontiguousarray(np.asarray(L0_ffn_up_w, dtype=f32).T)
    dwT = np.ascontiguousarray(np.asarray(L0_ffn_down_w, dtype=f32).T)
    gs = np.asarray(L0_ffn_gate_s, dtype=f32)
    us = np.asarray(L0_ffn_up_s, dtype=f32)
    ds = np.asarray(L0_ffn_down_s, dtype=f32)
    gs_cols = np.ascontiguousarray(gs.reshape(NIT, 128).T)
    us_cols = np.ascontiguousarray(us.reshape(NIT, 128).T)
    ds_bcast = np.ascontiguousarray(np.broadcast_to(ds[None, :], (128, DM)))

    if "nc" not in _CACHE:
        _CACHE["nc"] = build_program()
        _CACHE["consts"] = [_host_consts(hh) for hh in range(2)]
    nc = _CACHE["nc"]

    in_maps, metas = [], []
    for c in range(8):
        b, hh = c // 2, c % 2
        cc = _CACHE["consts"][hh]
        m = {"x_perm": np.ascontiguousarray(x[b][cc["tok_perm"]]),
             "wq_in": wq, "wk_in": wk, "wv_in": wv, "wo_in": wo,
             "gwT_in": gwT, "uwT_in": uwT, "dwT_in": dwT,
             "gs_cols": gs_cols, "us_cols": us_cols, "ds_bcast": ds_bcast}
        for k in ("cosq_in", "sinq_in", "cosk_in", "sink_in", "mask_in",
                  "perm2_in", "idb_in", "idf_in", "ones_in"):
            m[k] = cc[k]
        in_maps.append(m)
        metas.append((b, hh, cc["perm_tiles"]))

    res = run_bass_kernel_spmd(nc, in_maps, core_ids=list(range(8)))

    out = np.empty((B, S, DM), dtype=f32)
    for c in range(8):
        b, hh, perm_tiles = metas[c]
        yc = np.asarray(res.results[c]["y"]).reshape(16, 128, DM)
        for l, gt2 in enumerate(perm_tiles[:16]):
            out[b, gt2 * 128:(gt2 + 1) * 128, :] = yc[l]
    return out
